# version 21
# baseline (speedup 1.0000x reference)
import numpy as np
import concourse.bass as bass
import concourse.mybir as mybir
from concourse.bass_utils import run_bass_kernel_spmd
from contextlib import ExitStack

F32 = mybir.dt.float32
F32R = mybir.dt.float32r
BF16 = mybir.dt.bfloat16
AF = mybir.ActivationFunctionType
OP = mybir.AluOpType

T = 4096
D = 1024
E = 2048
N = 16
KD = D // 128
KE = E // 128
TC = 512
NCH = T // TC
EPS = 1e-5
POOL_NS = frozenset(range(10, 16))

ENGS = ("sync", "tensor", "scalar", "vector", "gpsimd")

TAPS_WANTED = set()
TAP_CHUNK = 0


SAME_ENG_TH = 10**9


class Plan:

    def __init__(self):
        self.ops = {e: [] for e in ENGS}
        self.marks = {e: [] for e in ENGS}
        self.counts = {}
        self.seen = {e: {} for e in ENGS}
        self.icnt = {e: 0 for e in ENGS}
        self.sem_names = []

    def sem(self, name):
        self.sem_names.append(name)
        self.counts[name] = 0
        return name

    def _resolve(self, eng, dep, cur_inum):
        peng, seq, sem, val, inum = dep
        if peng == eng and cur_inum - inum >= SAME_ENG_TH:
            return None
        if sem is not None:
            return (sem, val)
        marks = self.marks[peng]
        lo, hi = 0, len(marks)
        while lo < hi:
            mid = (lo + hi) // 2
            if marks[mid][0] >= seq:
                hi = mid
            else:
                lo = mid + 1
        if lo == len(marks):
            if peng == eng:
                return None
            raise RuntimeError(
                f"unresolvable cross-engine dep {peng}[{seq}] -> {eng}: "
                f"producer has no semaphore mark at/after seq {seq}")
        return (marks[lo][1], marks[lo][2])

    def emit(self, eng, thunk, reads=(), writes=(), inc=None, inc_by=1):
        deps = []
        for b in reads:
            if b.valid is not None:
                deps.append(b.valid)
        for b in writes:
            if b.readers:
                deps.extend(b.readers)
            elif b.valid is not None:
                deps.append(b.valid)
        cur_inum = self.icnt[eng]
        waits = {}
        for dep in deps:
            r = self._resolve(eng, dep, cur_inum)
            if r is None:
                continue
            sem, val = r
            if self.seen[eng].get(sem, 0) >= val:
                continue
            waits[sem] = max(waits.get(sem, 0), val)
        for sem, val in waits.items():
            self.seen[eng][sem] = val
        seq = len(self.ops[eng])
        self.icnt[eng] = cur_inum + 1
        if inc is not None:
            self.counts[inc] += inc_by
            self.marks[eng].append((seq, inc, self.counts[inc]))
            dep = (eng, seq, inc, self.counts[inc], cur_inum)
        else:
            dep = (eng, seq, None, 0, cur_inum)
        self.ops[eng].append((sorted(waits.items()), thunk, inc, inc_by))
        for b in reads:
            b.readers.append(dep)
        for b in writes:
            b.valid = dep
            b.readers = []
        return dep

    def raw_wait(self, eng, sem, val):
        if self.seen[eng].get(sem, 0) < val:
            self.seen[eng][sem] = val
            self.ops[eng].append(([(sem, val)], None, None, 0))


class Buf:
    __slots__ = ("valid", "readers", "name")

    def __init__(self, name=""):
        self.valid = None
        self.readers = []
        self.name = name


def build_program():
    nc = bass.Bass("TRN2", target_bir_lowering=False)
    p = Plan()
    es = ExitStack()

    def din(name, shape, dt=F32):
        return nc.dram_tensor(name, shape, dt, kind="ExternalInput").ap()

    xT = din("xT", [D, T])
    win = din("win", [32, 128, 8, 128])
    wout = din("wout", [8, 128, 16, 128])
    wr = din("wr", [8, 128, 8, 128])
    wk = din("wk", [8, 128, 8, 128])
    wv = din("wv", [8, 128, 8, 128])
    xp = din("xp", [128, 16 * 32])
    sel_in = din("sel", [17, 17 * 128])
    cdiag = din("cdiag", [128, 64 * 128])
    ddiag = din("ddiag", [128, 16 * 128])
    ident_in = din("ident", [128, 128])
    a_par = din("a_par", [128, 256])
    dtw_in = din("dtw", [128, 16])
    dtb_in = din("dtb", [128, 16])
    mur_in = din("mur", [128, 8])
    muk_in = din("muk", [128, 8])
    outb_in = din("outb", [128, 8])

    y_out = nc.dram_tensor("y_out", [D, T], F32, kind="ExternalOutput").ap()

    def scratch(name, shape):
        return nc.dram_tensor(name, shape, BF16, kind="Internal").ap()

    win_bf = scratch("win_bf", [32, 128, 8 * 128])
    wout_bf = scratch("wout_bf", [8, 128, 16 * 128])
    wr_bf = scratch("wr_bf", [8, 128, 8 * 128])
    wk_bf = scratch("wk_bf", [8, 128, 8 * 128])
    wv_bf = scratch("wv_bf", [8, 128, 8 * 128])
    cd_bf = scratch("cd_bf", [128, 64 * 128])
    dd_bf = scratch("dd_bf", [128, 16 * 128])

    def sb(name, shape, dt):
        return es.enter_context(nc.sbuf_tensor(name, shape, dt))

    s_xp = sb("s_xp", [128, 16 * 32], BF16)
    s_ident = sb("s_ident", [128, 128], F32R)
    s_ones128 = sb("s_ones128", [128, 128], BF16)
    s_A = sb("s_A", [128, 256], F32)
    s_dtw = sb("s_dtw", [128, 16], F32)
    s_dtb = sb("s_dtb", [128, 16], F32)
    s_mur = sb("s_mur", [128, 8], F32)
    s_muk = sb("s_muk", [128, 8], F32)
    s_cmur = sb("s_cmur", [128, 8], F32)
    s_cmuk = sb("s_cmuk", [128, 8], F32)
    s_outb = sb("s_outb", [128, 8], F32)
    s_const = sb("s_const", [128, 2], F32)
    s_winb = [sb(f"s_winb{i}", [128, 8 * 128], BF16) for i in range(2)]
    s_wzb = [sb(f"s_wzb{i}", [128, 8 * 128], BF16) for i in range(2)]
    s_woutb = [sb("s_woutb0", [128, 16 * 128], BF16)]
    s_ffnb = [sb(f"s_ffnb{i}", [128, 8 * 128], BF16) for i in range(2)]
    s_cdb = [sb(f"s_cdb{i}", [128, 4 * 128], BF16) for i in range(2)]
    s_ddb = [sb(f"s_ddb{i}", [128, 128], BF16) for i in range(2)]
    s_x = [[sb(f"s_x{pi}_{k}", [128, TC], F32) for k in range(KD)]
           for pi in range(2)]
    s_xbf = [[sb(f"s_xbf{pi}_{k}", [128, TC], BF16) for k in range(KD)]
             for pi in range(2)]
    s_upre = [sb(f"s_upre{i}", [128, TC + 4], BF16) for i in range(2)]
    s_utail = sb("s_utail", [128, 16 * 4], BF16)
    s_u = [sb(f"s_u{m}", [128, TC], BF16) for m in range(KE)]
    s_xdbl = sb("s_xdbl", [17, TC], F32R)
    s_sel = sb("s_sel", [17, 17 * 128], F32R)
    s_dtin = sb("s_dtin", [128, TC], F32)
    s_B = [sb(f"s_B{n}", [128, TC], F32) for n in range(N)]
    s_dl = [sb(f"s_dl{i}", [128, TC], F32) for i in range(4)]
    s_a = [sb(f"s_a{i}", [128, TC], F32) for i in range(2)]
    s_b = [sb(f"s_b{i}", [128, TC], F32) for i in range(4)]
    s_h = [sb(f"s_h{i}", [128, TC], F32) for i in range(6)]
    s_hb = [sb(f"s_hb{i}", [128, TC], F32R) for i in range(2)]
    s_state = sb("s_state", [128, 256], F32)
    s_sz = [sb(f"s_sz{i}", [128, TC], F32) for i in range(2)]
    s_x1 = [sb(f"s_x1{k}", [128, TC + 1], F32) for k in range(KD)]
    s_xr = [sb(f"s_xr{k}", [128, TC], BF16) for k in range(KD)]
    s_xk = [sb(f"s_xk{k}", [128, TC], BF16) for k in range(KD)]

    def psum(name, shape=(128, TC)):
        return es.enter_context(nc.psum_tensor(name, list(shape), F32))

    p_A = [psum("p_A0"), psum("p_A1")]
    p_C = psum("p_C")
    p_X = psum("p_X")
    p_Bc = psum("p_Bc")
    p_Y = [psum("p_Y0"), psum("p_Y1")]
    p_Z = psum("p_Z")

    SM_DMA = p.sem("sm_dma")
    SM_OUT = p.sem("sm_out")
    DSEMS = {}
    for nm in (["d_win0", "d_win1", "d_wz0", "d_wz1", "d_wout", "d_ffn0",
                "d_ffn1", "d_cd0", "d_cd1", "d_dd0", "d_dd1", "d_x0", "d_x1"]
               + [f"d_st{i}" for i in range(8)] + ["d_out0", "d_out1"]):
        DSEMS[nm] = p.sem(nm)
    SM_PE = p.sem("sm_pe")
    SM_ACT = p.sem("sm_act")
    SM_DVE = p.sem("sm_dve")
    SM_POOL = p.sem("sm_pool")
    ESEM = {"tensor": SM_PE, "scalar": SM_ACT, "vector": SM_DVE,
            "gpsimd": SM_POOL}

    B_ = {}

    def buf(key):
        if key not in B_:
            B_[key] = Buf(str(key))
        return B_[key]

    def dma(src_ap, dst_ap, reads=(), writes=(), sem=SM_DMA, q="sync"):
        p.raw_wait(q, sem, p.counts[sem])
        def th(eng, s=src_ap, d=dst_ap):
            return eng.dma_start(out=d, in_=s)
        return p.emit(q, th, reads=reads, writes=writes, inc=sem, inc_by=16)

    def tap(name, ap, reads, chunk):
        if name not in TAPS_WANTED or chunk != TAP_CHUNK:
            return
        t = nc.dram_tensor("tap_" + name, list(ap.shape), ap.dtype,
                           kind="ExternalOutput").ap()
        dma(ap, t, reads=reads, writes=(), sem=SM_OUT)

    def act(fn, out_ap, in_ap, reads, writes, scale=1.0, bias=0.0):
        def th(eng, o=out_ap, i=in_ap, f=fn, s=scale, b=bias):
            return eng.activation(o, i, f, bias=b, scale=s)
        return p.emit("scalar", th, reads=reads, writes=writes, inc=SM_ACT)

    def vop(engn, th, reads, writes, inc=True):
        return p.emit(engn, th, reads=reads, writes=writes,
                      inc=ESEM[engn] if inc else None)

    def mmgroup(bank_buf, bank_ap, items):
        n = len(items)
        for i, (l, r, rd) in enumerate(items):
            def th(eng, o=bank_ap, l=l, r=r, st=(i == 0), sp=(i == n - 1)):
                return eng.matmul(o, l, r, start=st, stop=sp,
                                  skip_group_check=True)
            p.emit("tensor", th, reads=rd,
                   writes=[bank_buf] if (i == 0 or i == n - 1) else (),
                   inc=SM_PE if i == n - 1 else None)

    small_loads = [("a_par", a_par, s_A), ("dtw", dtw_in, s_dtw),
                   ("dtb", dtb_in, s_dtb), ("mur", mur_in, s_mur),
                   ("muk", muk_in, s_muk), ("outb", outb_in, s_outb)]
    for name, src, dst in small_loads:
        dma(src, dst[:], writes=[buf(name)])
    act(AF.Exp, s_A[:], s_A[:], reads=[buf("a_par")], writes=[buf("a_par")])
    vop("vector", (lambda eng: eng.tensor_scalar(
        out=s_A[:], in0=s_A[:], scalar1=-1.0, scalar2=None, op0=OP.mult)),
        reads=[buf("a_par")], writes=[buf("a_par")])
    vop("vector", (lambda eng: eng.tensor_scalar(
        out=s_cmur[:], in0=s_mur[:], scalar1=-1.0, scalar2=1.0,
        op0=OP.mult, op1=OP.add)), reads=[buf("mur")], writes=[buf("cmur")])
    vop("vector", (lambda eng: eng.tensor_scalar(
        out=s_cmuk[:], in0=s_muk[:], scalar1=-1.0, scalar2=1.0,
        op0=OP.mult, op1=OP.add)), reads=[buf("muk")], writes=[buf("cmuk")])
    for off in range(0, 17 * 128, 512):
        w = min(512, 17 * 128 - off)
        lb = buf(("b", 0))
        dma(sel_in[:, off:off + w], s_b[0][:17, :w], writes=[lb])
        vop("vector", (lambda eng, off=off, w=w: eng.tensor_copy(
            out=s_sel[:, off:off + w], in_=s_b[0][:17, :w])),
            reads=[lb], writes=[buf("sel")])
    vop("vector", (lambda eng: eng.memset(s_const[:, 0:1], EPS)), reads=(),
        writes=[buf("const")])
    vop("vector", (lambda eng: eng.memset(s_const[:, 1:2], -0.5)), reads=(),
        writes=[buf("const")])
    vop("vector", (lambda eng: eng.memset(s_ones128[:], 1.0)), reads=(),
        writes=[buf("ones128")])
    vop("vector", (lambda eng: eng.memset(s_utail[:], 0.0)), reads=(),
        writes=[buf(("utail", m)) for m in range(KE)])
    vop("vector", (lambda eng: eng.memset(s_state[:], 0.0)), reads=(),
        writes=[buf(("st", i)) for i in range(256)])
    for k in range(KD):
        vop("vector", (lambda eng, k=k: eng.memset(s_x1[k][:, 0:1], 0.0)),
            reads=(), writes=[buf(("x1", k))])

    lb = buf(("b", 0))
    dma(ident_in, s_b[0][:, :128], writes=[lb])
    vop("vector", (lambda eng: eng.tensor_copy(
        out=s_ident[:], in_=s_b[0][:, :128])), reads=[lb], writes=[buf("ident")])

    lb = buf(("b", 0))
    dma(xp[:, 0:512], s_b[0][:, 0:512], writes=[lb])
    vop("vector", (lambda eng: eng.tensor_copy(
        out=s_xp[:], in_=s_b[0][:, 0:512])), reads=[lb], writes=[buf("xp")])

    _pp_slot = [0]
    WBF = {}

    def cast_blocks(name, blocks):
        for srcap, dstap, w, key in blocks:
            i = _pp_slot[0] % 8
            _pp_slot[0] += 1
            fb = buf(("x", 1, i))
            bb = buf(("xbf", 1, i))
            dma(srcap, s_x[1][i][:, :w], writes=[fb], sem=DSEMS[f"d_st{i}"])
            act(AF.Copy, s_xbf[1][i][:, :w], s_x[1][i][:, :w],
                reads=[fb], writes=[bb])
            dma(s_xbf[1][i][:, :w], dstap, reads=[bb],
                writes=[buf(key)], sem=DSEMS[f"d_st{i}"])

    def wblocks(name, src_f32, dst_bf, nblk, blkw):
        out = []
        for i in range(nblk):
            for off in range(0, blkw, TC):
                w = min(TC, blkw - off)
                s = (src_f32[i].rearrange("p a b -> p (a b)")[:, off:off + w]
                     if src_f32.ndim == 4 else
                     src_f32[:, i * blkw + off:i * blkw + off + w])
                d = (dst_bf[i][:, off:off + w] if dst_bf.ndim == 3
                     else dst_bf[:, i * blkw + off:i * blkw + off + w])
                key = ("wbf", name, i, off)
                WBF.setdefault((name, i), []).append(key)
                out.append((s, d, w, key))
        return out

    cast_blocks("win", wblocks("win", win, win_bf, 32, 1024))
    for o in range(0, 64 * 128, TC):
        WBF.setdefault(("cd", o // TC), []).append(("wbf", "cd", o // TC))
    cast_blocks("cd", [(cdiag[:, o:o + TC], cd_bf[:, o:o + TC], TC,
                        ("wbf", "cd", o // TC)) for o in range(0, 64 * 128, TC)])
    for o in range(0, 16 * 128, TC):
        WBF.setdefault(("dd", o // TC), []).append(("wbf", "dd", o // TC))
    cast_blocks("dd", [(ddiag[:, o:o + TC], dd_bf[:, o:o + TC], TC,
                        ("wbf", "dd", o // TC)) for o in range(0, 16 * 128, TC)])
    cast_blocks("wout", wblocks("wout", wout, wout_bf, 8, 2048))
    cast_blocks("wr", wblocks("wr", wr, wr_bf, 8, 1024))
    cast_blocks("wk", wblocks("wk", wk, wk_bf, 8, 1024))
    cast_blocks("wv", wblocks("wv", wv, wv_bf, 8, 1024))

    for c in range(NCH):
        t0 = c * TC
        pi = c % 2
        sx = s_x[pi]
        sxbf = s_xbf[pi]
        for k in range(KD):
            dma(xT[k * 128:(k + 1) * 128, t0:t0 + TC], sx[k][:],
                writes=[buf(("x", pi, k))], sem=DSEMS[f"d_x{pi}"])
            act(AF.Copy, sxbf[k][:], sx[k][:],
                reads=[buf(("x", pi, k))], writes=[buf(("xbf", pi, k))])

        for m in range(KE):
            wb = buf(("winb", m % 2))
            dma(win_bf[m], s_winb[m % 2][:], writes=[wb],
                reads=[buf(k) for k in WBF[("win", m)]] if c == 0 else (),
                sem=DSEMS[f"d_win{m % 2}"])
            cdb = buf(("cdb", m % 2))
            dma(cd_bf[:, m * 512:(m + 1) * 512], s_cdb[m % 2][:], writes=[cdb],
                reads=[buf(k) for k in WBF[("cd", m)]] if c == 0 else (),
                sem=DSEMS[f"d_cd{m % 2}"])
            bank, pb = p_A[m % 2], buf(("pA", m % 2))
            mmgroup(pb, bank[:],
                    [(s_winb[m % 2][:, k * 128:(k + 1) * 128], sxbf[k][:],
                      [wb, buf(("xbf", pi, k))]) for k in range(KD)])
            ring, rb = s_upre[m % 2], buf(("upre", m % 2))
            rtb = buf(("upre_t", m % 2))
            if m == 0:
                tap("utail_pre0", s_utail[:, 0:4], [buf(("utail", 0))], c)
            act(AF.Copy, ring[:, 0:4], s_utail[:, m * 4:m * 4 + 4],
                reads=[buf(("utail", m))], writes=[rtb])
            act(AF.Copy, ring[:, 4:4 + TC], bank[:], reads=[pb], writes=[rb])
            vop("gpsimd", (lambda eng, r=ring, m=m: eng.tensor_copy(
                out=s_utail[:, m * 4:m * 4 + 4], in_=r[:, TC:TC + 4])),
                reads=[rb], writes=[buf(("utail", m))])
            cb = buf("pC")
            mmgroup(cb, p_C[:],
                    [(s_cdb[m % 2][:, kk * 128:(kk + 1) * 128],
                      ring[:, kk + 1:kk + 1 + TC], [cdb, rb, rtb])
                     for kk in range(4)])
            ub = buf(("u", m))
            act(AF.Silu, s_u[m][:], p_C[:], reads=[cb], writes=[ub])
            tap(f"u{m}", s_u[m][:], [ub], c)
            tap(f"upre{m}", ring[:, 0:TC + 4], [rb, rtb], c)
        xpb = buf("pX")
        mmgroup(xpb, p_X[0:17, :],
                [(s_xp[:, m * 32:m * 32 + 17], s_u[m][:],
                  [buf(("u", m)), buf("xp")]) for m in range(KE)])
        xdb = buf("xdbl")
        vop("vector", (lambda eng: eng.tensor_copy(
            out=s_xdbl[:], in_=p_X[0:17, :])), reads=[xpb], writes=[xdb])
        tap("xdbl", s_xdbl[:], [xdb], c)
        if "xdblf" in TAPS_WANTED and c == TAP_CHUNK:
            vop("vector", (lambda eng: eng.tensor_copy(
                out=s_a[0][:17, :], in_=p_X[0:17, :])), reads=[xpb],
                writes=[buf(("a", 0))])
            tap("xdblf", s_a[0][:17, :], [buf(("a", 0))], c)
        if "sxp" in TAPS_WANTED and c == TAP_CHUNK:
            tap("sxp", s_xp[:], [buf("xp")], c)
        bcb = buf("pBc")
        for r in range(17):
            mmgroup(bcb, p_Bc[:],
                    [(s_sel[:, r * 128:(r + 1) * 128], s_xdbl[:],
                      [xdb, buf("sel")])])
            dstap = s_dtin[:] if r == 0 else s_B[r - 1][:]
            dstbuf = buf("dtin") if r == 0 else buf(("B", r - 1))
            vop("vector", (lambda eng, d=dstap: eng.tensor_copy(
                out=d, in_=p_Bc[:])), reads=[bcb], writes=[dstbuf])
            if r == 0:
                tap("dtin", s_dtin[:], [dstbuf], c)
            else:
                tap(f"B{r-1}", s_B[r - 1][:], [dstbuf], c)

        for m in range(KE):
            sl = m % 2
            dsl = m % 4
            def z_group(mm_):
                zbank, zkey = (p_Z, "pZ") if mm_ % 2 == 0 else (p_Bc, "pBc")
                wzbuf = buf(("wzb", mm_ % 2))
                dma(win_bf[16 + mm_], s_wzb[mm_ % 2][:], writes=[wzbuf],
                    reads=[buf(k) for k in WBF[("win", 16 + mm_)]] if c == 0 else (),
                    sem=DSEMS[f"d_wz{mm_ % 2}"])
                zb = buf(zkey)
                mmgroup(zb, zbank[:],
                        [(s_wzb[mm_ % 2][:, k * 128:(k + 1) * 128],
                          sxbf[k][:],
                          [wzbuf, buf(("xbf", pi, k))]) for k in range(KD)])

            def z_silu(mm_):
                zbank, zkey = (p_Z, "pZ") if mm_ % 2 == 0 else (p_Bc, "pBc")
                szb2 = buf(("sz", mm_ % 2))
                act(AF.Silu, s_sz[mm_ % 2][:], zbank[:], reads=[buf(zkey)],
                    writes=[szb2])
                tap(f"sz{mm_}", s_sz[mm_ % 2][:], [szb2], c)

            if m == 0:
                for mm_ in (0, 1):
                    z_group(mm_)
                    z_silu(mm_)
            szb = buf(("sz", m % 2))

            def emit_delta(mm_, phase):
                dsl_ = mm_ % 4
                db_ = buf(("dl", dsl_))
                if phase is AF.Exp:
                    act(AF.Exp, s_dl[dsl_][:], s_dtin[:],
                        scale=s_dtw[:, mm_:mm_ + 1], bias=s_dtb[:, mm_:mm_ + 1],
                        reads=[buf("dtin"), buf("dtw"), buf("dtb")],
                        writes=[db_])
                else:
                    act(AF.Ln, s_dl[dsl_][:], s_dl[dsl_][:], bias=1.0,
                        reads=[db_], writes=[db_])
                    tap(f"dl{mm_}", s_dl[dsl_][:], [db_], c)

            if m == 0:
                for mm_ in (0, 1):
                    emit_delta(mm_, AF.Exp)
                for mm_ in (0, 1):
                    emit_delta(mm_, AF.Ln)
            dlb = buf(("dl", dsl))
            ddb = buf(("ddb", sl))
            dma(dd_bf[:, m * 128:(m + 1) * 128], s_ddb[sl][:], writes=[ddb],
                reads=[buf(k) for k in WBF[("dd", m // 4)]] if c == 0 else (),
                sem=DSEMS[f"d_dd{sl}"])
            ybank, yb = p_Y[sl], buf(("pY", sl))
            def th_seed(eng, o=ybank[:], l=s_ddb[sl][:], r=s_u[m][:]):
                return eng.matmul(o, l, r, start=True, stop=False,
                                  skip_group_check=True)
            p.emit("tensor", th_seed, reads=[ddb, buf(("u", m))],
                   writes=[yb], inc=None)
            if m % 2 == 0 and m + 2 < KE:
                z_group(m + 2)
                z_group(m + 3)
            pa_slots = [p_A[0], p_A[1], p_C, p_X]
            pa_keys = [("pA", 0), ("pA", 1), "pC", "pX"]

            g2 = 2 * (c * KE + m)

            def emit_hb_sum(n, eng):
                hs_, hbb = (n + g2) % 6, buf(("hb", n % 2))
                vop(eng, (lambda e, n=n, hs=hs_: e.tensor_tensor(
                    out=s_hb[n % 2][:], in0=s_h[hs][:], in1=s_B[n][:],
                    op=OP.mult)),
                    reads=[buf(("h", hs_)), buf(("B", n))], writes=[hbb])
                last = (n == N - 1)
                def th_sum(eng2, o=ybank[:], l=s_ident[:],
                           r=s_hb[n % 2][:], sp=last):
                    return eng2.matmul(o, l, r, start=False, stop=sp,
                                       skip_group_check=True)
                p.emit("tensor", th_sum, reads=[hbb, buf("ident")],
                       writes=[yb] if last else (), inc=SM_PE)

            for n in range(N):
                ns = n % 4
                ab = buf(pa_keys[ns])
                act(AF.Exp, pa_slots[ns][:], s_dl[dsl][:],
                    scale=s_A[:, m * 16 + n:m * 16 + n + 1],
                    reads=[dlb, buf("a_par")], writes=[ab])
                if n == 0 and m + 2 < KE:
                    emit_delta(m + 2, AF.Exp)
                if n == 5 and m + 2 < KE:
                    emit_delta(m + 2, AF.Ln)
                bb = buf(("b", ns))
                vop("gpsimd", (lambda eng, ns=ns, n=n, dsl=dsl: eng.tensor_tensor(
                    out=s_b[ns][:], in0=s_dl[dsl][:], in1=s_B[n][:], op=OP.mult)),
                    reads=[dlb, buf(("B", n))], writes=[bb])
                idx = m * 16 + n
                hs = (n + g2) % 6
                hbuf = buf(("h", hs))
                vop("vector", (lambda eng, ns=ns, hs=hs, idx=idx:
                               eng.tensor_tensor_scan(
                    out=s_h[hs][:], data0=pa_slots[ns][:], data1=s_b[ns][:],
                    initial=s_state[:, idx:idx + 1], op0=OP.mult, op1=OP.add)),
                    reads=[ab, bb, buf(("st", idx))], writes=[hbuf])
                if m == 0 and n == 0:
                    tap("h00", s_h[hs][:], [hbuf], c)
                    tap("b00", s_b[0][:], [bb], c)
                dma(s_h[hs][:, TC - 1:TC], s_state[:, idx:idx + 1],
                    reads=[hbuf], writes=[buf(("st", idx))],
                    sem=DSEMS[f"d_st{idx % 8}"])
                if n >= 3 and n - 3 <= 12:
                    emit_hb_sum(n - 3, "vector")
            emit_hb_sum(13, "vector")
            emit_hb_sum(14, "gpsimd")
            emit_hb_sum(15, "gpsimd")
            ygb = buf(("u", m))
            vop("vector", (lambda eng, m=m, ybank=ybank: eng.tensor_tensor(
                out=s_u[m][:], in0=ybank[:], in1=s_sz[m % 2][:], op=OP.mult)),
                reads=[yb, szb], writes=[ygb])
            tap(f"yg{m}", s_u[m][:], [ygb], c)
            if m % 2 == 1 and m + 2 < KE:
                z_silu(m + 1)
                z_silu(m + 2)

        for k in range(KD):
            wb = buf(("woutb", 0))
            dma(wout_bf[k], s_woutb[0][:], writes=[wb],
                reads=[buf(kk) for kk in WBF[("wout", k)]] if c == 0 else (),
                sem=DSEMS["d_wout"])
            bank, pb = p_A[k % 2], buf(("pA", k % 2))
            mmgroup(pb, bank[:],
                    [(s_woutb[0][:, j * 128:(j + 1) * 128], s_u[j][:],
                      [wb, buf(("u", j))]) for j in range(KE)])
            vop("vector", (lambda eng, k=k, bank=bank, sx=sx:
                           eng.scalar_tensor_tensor(
                out=sx[k][:], in0=bank[:], scalar=s_outb[:, k:k + 1],
                in1=sx[k][:], op0=OP.add, op1=OP.add)),
                reads=[pb, buf(("x", pi, k)), buf("outb")],
                writes=[buf(("x", pi, k))])
            tap(f"xm{k}", sx[k][:], [buf(("x", pi, k))], c)

        def layernorm(src_bufs, src_aps, dst_writer):
            for k in range(KD):
                act(AF.Copy, sxbf[k][:], src_aps[k],
                    reads=[src_bufs[k]], writes=[buf(("xbf", pi, k))])
                act(AF.Square, s_xr[k][:], src_aps[k],
                    reads=[src_bufs[k]], writes=[buf(("xr", k))])
            mmgroup(buf("pBc"), p_Bc[:],
                    [(s_ones128[:], sxbf[k][:], [buf(("xbf", pi, k)),
                                                 buf("ones128")])
                     for k in range(KD)])
            mmgroup(buf("pZ"), p_Z[:],
                    [(s_ones128[:], s_xr[k][:], [buf(("xr", k)), buf("ones128")])
                     for k in range(KD)])
            mb = buf(("b", 0))
            vop("vector", (lambda eng: eng.tensor_scalar(
                out=s_b[0][:], in0=p_Bc[:], scalar1=1.0 / D, scalar2=None,
                op0=OP.mult)), reads=[buf("pBc")], writes=[mb])
            m2b = buf(("b", 1))
            vop("vector", (lambda eng: eng.tensor_tensor(
                out=s_b[1][:], in0=s_b[0][:], in1=s_b[0][:], op=OP.mult)),
                reads=[mb], writes=[m2b])
            vop("vector", (lambda eng: eng.scalar_tensor_tensor(
                out=s_b[1][:], in0=p_Z[:], scalar=1.0 / D, in1=s_b[1][:],
                op0=OP.mult, op1=OP.subtract)),
                reads=[buf("pZ"), m2b], writes=[m2b])
            ib = buf(("sz", 0))
            act(AF.Ln, s_sz[0][:], s_b[1][:], bias=s_const[:, 0:1],
                reads=[m2b, buf("const")], writes=[ib])
            act(AF.Exp, s_sz[0][:], s_sz[0][:], scale=s_const[:, 1:2],
                reads=[ib, buf("const")], writes=[ib])
            for k in range(KD):
                vop("gpsimd", (lambda eng, k=k, a=src_aps[k]: eng.tensor_tensor(
                    out=a, in0=a, in1=s_b[0][:], op=OP.subtract)),
                    reads=[src_bufs[k], mb], writes=[src_bufs[k]])
            for k in range(KD):
                dst_writer(k, src_bufs[k], ib)

        def w1(k, xcb, ib):
            vop("gpsimd", (lambda eng, k=k, sx=sx: eng.tensor_tensor(
                out=s_x1[k][:, 1:1 + TC], in0=sx[k][:], in1=s_sz[0][:],
                op=OP.mult)), reads=[xcb, ib], writes=[buf(("x1", k))])
            tap(f"x1_{k}", s_x1[k][:, 1:1 + TC], [buf(("x1", k))], c)
        layernorm([buf(("x", pi, k)) for k in range(KD)],
                  [sx[k][:] for k in range(KD)], w1)

        mix_slots = [(s_a[0], ("a", 0)), (s_a[1], ("a", 1)),
                     (s_b[2], ("b", 2)), (s_b[3], ("b", 3)),
                     (s_dl[0], ("dl", 0)), (s_dl[1], ("dl", 1)),
                     (s_dl[2], ("dl", 2)), (s_dl[3], ("dl", 3))]
        mix_items = [(k, j) for k in range(KD) for j in (0, 1)]

        def mix_ts(i):
            k, j = mix_items[i]
            st, skey = mix_slots[i % 8]
            cmu = s_cmur if j == 0 else s_cmuk
            cbuf = buf("cmur") if j == 0 else buf("cmuk")
            vop("vector", (lambda eng, k=k, cmu=cmu, st=st: eng.tensor_scalar(
                out=st[:], in0=s_x1[k][:, 0:TC],
                scalar1=cmu[:, k:k + 1], scalar2=None, op0=OP.mult)),
                reads=[buf(("x1", k)), cbuf], writes=[buf(skey)])

        def mix_stt(i):
            k, j = mix_items[i]
            st, skey = mix_slots[i % 8]
            mu = s_mur if j == 0 else s_muk
            mbuf = buf("mur") if j == 0 else buf("muk")
            dst, tag = (s_xr, "xr") if j == 0 else (s_xk, "xk")
            vop("vector", (lambda eng, k=k, mu=mu, dst=dst, st=st:
                           eng.scalar_tensor_tensor(
                out=dst[k][:], in0=s_x1[k][:, 1:1 + TC],
                scalar=mu[:, k:k + 1], in1=st[:],
                op0=OP.mult, op1=OP.add)),
                reads=[buf(("x1", k)), buf(skey), mbuf], writes=[buf((tag, k))])

        for i in range(len(mix_items)):
            mix_ts(i)
            if i >= 4:
                mix_stt(i - 4)
        for i in range(len(mix_items) - 4, len(mix_items)):
            mix_stt(i)

        for k in range(KD):
            wb = buf(("ffnb", k % 2))
            dma(wk_bf[k], s_ffnb[k % 2][:], writes=[wb],
                reads=[buf(kk) for kk in WBF[("wk", k)]] if c == 0 else (),
                sem=DSEMS[f"d_ffn{k % 2}"])
            bank, pb = p_A[k % 2], buf(("pA", k % 2))
            mmgroup(pb, bank[:],
                    [(s_ffnb[k % 2][:, j * 128:(j + 1) * 128], s_xk[j][:],
                      [wb, buf(("xk", j))]) for j in range(KD)])
            krb = buf(("a", k % 2))
            vop("vector", (lambda eng, k=k, bank=bank: eng.tensor_scalar(
                out=s_a[k % 2][:], in0=bank[:], scalar1=0.0, scalar2=None,
                op0=OP.max)), reads=[pb], writes=[krb])
            vop("gpsimd", (lambda eng, k=k: eng.tensor_tensor(
                out=s_u[k][:], in0=s_a[k % 2][:], in1=s_a[k % 2][:],
                op=OP.mult)), reads=[krb], writes=[buf(("u", k))])

        def emit_x2m_add(kk2):
            vop("vector", (lambda eng, k=kk2: eng.tensor_tensor(
                out=s_x1[k][:, 1:1 + TC], in0=s_dl[k % 4][:],
                in1=s_x1[k][:, 1:1 + TC], op=OP.add)),
                reads=[buf(("dl", kk2 % 4)), buf(("x1", kk2))],
                writes=[buf(("x1", kk2))])

        for k in range(KD):
            wbr = buf(("ffnb", k % 2))
            dma(wr_bf[k], s_ffnb[k % 2][:], writes=[wbr],
                reads=[buf(kk) for kk in WBF[("wr", k)]] if c == 0 else (),
                sem=DSEMS[f"d_ffn{k % 2}"])
            rb = buf("pZ")
            mmgroup(rb, p_Z[:],
                    [(s_ffnb[k % 2][:, j * 128:(j + 1) * 128], s_xr[j][:],
                      [wbr, buf(("xr", j))]) for j in range(KD)])
            sgb = buf(("dl", k % 4))
            act(AF.Sigmoid, s_dl[k % 4][:], p_Z[:], reads=[rb], writes=[sgb])
            wbv = buf(("winb", k % 2))
            dma(wv_bf[k], s_winb[k % 2][:], writes=[wbv],
                reads=[buf(kk) for kk in WBF[("wv", k)]] if c == 0 else (),
                sem=DSEMS[f"d_win{k % 2}"])
            bankv, pvb = p_A[k % 2], buf(("pA", k % 2))
            mmgroup(pvb, bankv[:],
                    [(s_winb[k % 2][:, j * 128:(j + 1) * 128], s_u[j][:],
                      [wbv, buf(("u", j))]) for j in range(KD)])
            vop("vector", (lambda eng, k=k, bankv=bankv: eng.tensor_tensor(
                out=s_dl[k % 4][:], in0=s_dl[k % 4][:], in1=bankv[:],
                op=OP.mult)), reads=[sgb, pvb], writes=[sgb])
            vop("vector", (lambda eng, k=k: eng.tensor_copy(
                out=s_x1[k][:, 0:1], in_=s_x1[k][:, TC:TC + 1])),
                reads=[buf(("x1", k))], writes=[buf(("x1", k))])
            if k >= 2:
                emit_x2m_add(k - 2)
        emit_x2m_add(KD - 2)
        emit_x2m_add(KD - 1)

        def w2(k, xcb, ib):
            ob = buf(("h", k % 2))
            vop("vector", (lambda eng, k=k: eng.tensor_tensor(
                out=s_h[k % 2][:], in0=s_x1[k][:, 1:1 + TC], in1=s_sz[0][:],
                op=OP.mult)), reads=[xcb, ib], writes=[ob])
            dma(s_h[k % 2][:], y_out[k * 128:(k + 1) * 128, t0:t0 + TC],
                reads=[ob], writes=(), sem=DSEMS[f"d_out{k % 2}"])
        layernorm([buf(("x1", k)) for k in range(KD)],
                  [s_x1[k][:, 1:1 + TC] for k in range(KD)], w2)

    p.raw_wait("sync", DSEMS["d_out0"], p.counts[DSEMS["d_out0"]])
    p.raw_wait("sync", DSEMS["d_out1"], p.counts[DSEMS["d_out1"]])
    p.raw_wait("sync", SM_OUT, p.counts[SM_OUT])

    sems = {nm: es.enter_context(nc.semaphore(nm)) for nm in p.sem_names}

    with nc.Block() as blk:
        def playback(engname):
            def run(eng):
                for waits, thunk, inc, inc_by in p.ops[engname]:
                    for snm, val in waits:
                        eng.wait_ge(sems[snm], val)
                    if thunk is None:
                        continue
                    instr = thunk(eng)
                    if inc is not None:
                        instr.then_inc(sems[inc], inc_by)
            return run
        blk.sync(playback("sync"))
        blk.tensor(playback("tensor"))
        blk.scalar(playback("scalar"))
        blk.vector(playback("vector"))
        blk.gpsimd(playback("gpsimd"))

    es.close()
    return nc


def prep_core_inputs(inputs, c):
    f32 = np.float32
    x = np.ascontiguousarray(np.asarray(inputs["L0_attn_x"][c], f32).T)
    ip = np.asarray(inputs["L0_attn_in_proj_w"], f32)
    win = np.ascontiguousarray(ip.reshape(32, 128, 8, 128).transpose(0, 3, 2, 1))
    ow = np.asarray(inputs["L0_attn_out_w"], f32)
    wout = np.ascontiguousarray(ow.reshape(8, 128, 16, 128).transpose(0, 3, 2, 1))

    def ffnw(name):
        w = np.asarray(inputs[name], f32)
        return np.ascontiguousarray(w.reshape(8, 128, 8, 128).transpose(0, 3, 2, 1))

    xpw = np.asarray(inputs["L0_attn_x_proj_w"], f32)
    xp3 = np.zeros((128, 16, 32), f32)
    xp3[:, :, :17] = xpw.T.reshape(16, 128, 17).transpose(1, 0, 2)
    xp = np.ascontiguousarray(xp3.reshape(128, 16 * 32))
    cw = np.asarray(inputs["L0_attn_conv_w"], f32).reshape(E, 4)
    cd = np.zeros((128, 64, 128), f32)
    for e_hi in range(16):
        for kk in range(4):
            np.einsum('ii->i', cd[:, e_hi * 4 + kk, :])[:] = \
                cw[e_hi * 128:(e_hi + 1) * 128, kk]
    dp = np.asarray(inputs["L0_attn_D_param"], f32)
    dd = np.zeros((128, 16, 128), f32)
    for e_hi in range(16):
        np.einsum('ii->i', dd[:, e_hi, :])[:] = dp[e_hi * 128:(e_hi + 1) * 128]
    ap_ = np.asarray(inputs["L0_attn_A_param"], f32)
    a_par = np.ascontiguousarray(
        ap_.reshape(16, 128, 16).transpose(1, 0, 2).reshape(128, 256))

    def col16(v):
        return np.ascontiguousarray(np.asarray(v, f32).reshape(16, 128).T)

    def col8(v):
        return np.ascontiguousarray(np.asarray(v, f32).reshape(8, 128).T)

    sel = np.zeros((17, 17, 128), f32)
    for r in range(17):
        sel[r, r, :] = 1.0
    sel = np.ascontiguousarray(sel.transpose(1, 0, 2).reshape(17, 17 * 128))
    return {
        "sel": sel,
        "xT": x, "win": win, "wout": wout,
        "wr": ffnw("L0_ffn_W_r"), "wk": ffnw("L0_ffn_W_k"),
        "wv": ffnw("L0_ffn_W_v"),
        "xp": xp,
        "cdiag": np.ascontiguousarray(cd.reshape(128, 64 * 128)),
        "ddiag": np.ascontiguousarray(dd.reshape(128, 16 * 128)),
        "ident": np.eye(128, dtype=f32),
        "a_par": a_par,
        "dtw": col16(np.asarray(inputs["L0_attn_dt_w"], f32)[:, 0]),
        "dtb": col16(inputs["L0_attn_dt_b"]),
        "mur": col8(inputs["L0_ffn_mu_r"]),
        "muk": col8(inputs["L0_ffn_mu_k"]),
        "outb": col8(inputs["L0_attn_out_b"]),
    }


def kernel(**inputs):
    nc = build_program()
    in_maps = [prep_core_inputs(inputs, c) for c in range(8)]
    res = run_bass_kernel_spmd(nc, in_maps, core_ids=list(range(8)))
    out = np.stack([np.asarray(r["y_out"]) for r in res.results])
    return np.ascontiguousarray(out.transpose(0, 2, 1)).astype(np.float32)


# version 23 (final)
# speedup vs baseline: 1.1737x; 1.1737x over previous
import numpy as np
import concourse.bass as bass
import concourse.mybir as mybir
from concourse.bass_utils import run_bass_kernel_spmd
from contextlib import ExitStack

F32 = mybir.dt.float32
F32R = mybir.dt.float32r
BF16 = mybir.dt.bfloat16
AF = mybir.ActivationFunctionType
OP = mybir.AluOpType

T = 4096
D = 1024
E = 2048
N = 16
KD = D // 128
KE = E // 128
TC = 512
NCH = T // TC
EPS = 1e-5
POOL_NS = frozenset(range(10, 16))

ENGS = ("sync", "tensor", "scalar", "vector", "gpsimd")

TAPS_WANTED = set()
TAP_CHUNK = 0


SAME_ENG_TH = 10**9


class Plan:

    def __init__(self):
        self.ops = {e: [] for e in ENGS}
        self.marks = {e: [] for e in ENGS}
        self.counts = {}
        self.seen = {e: {} for e in ENGS}
        self.icnt = {e: 0 for e in ENGS}
        self.sem_names = []

    def sem(self, name):
        self.sem_names.append(name)
        self.counts[name] = 0
        return name

    def _resolve(self, eng, dep, cur_inum):
        peng, seq, sem, val, inum = dep
        if peng == eng and cur_inum - inum >= SAME_ENG_TH:
            return None
        if sem is not None:
            return (sem, val)
        marks = self.marks[peng]
        lo, hi = 0, len(marks)
        while lo < hi:
            mid = (lo + hi) // 2
            if marks[mid][0] >= seq:
                hi = mid
            else:
                lo = mid + 1
        if lo == len(marks):
            if peng == eng:
                return None
            raise RuntimeError(
                f"unresolvable cross-engine dep {peng}[{seq}] -> {eng}: "
                f"producer has no semaphore mark at/after seq {seq}")
        return (marks[lo][1], marks[lo][2])

    def emit(self, eng, thunk, reads=(), writes=(), inc=None, inc_by=1):
        deps = []
        for b in reads:
            if b.valid is not None:
                deps.append(b.valid)
        for b in writes:
            if b.readers:
                deps.extend(b.readers)
            elif b.valid is not None:
                deps.append(b.valid)
        cur_inum = self.icnt[eng]
        waits = {}
        for dep in deps:
            r = self._resolve(eng, dep, cur_inum)
            if r is None:
                continue
            sem, val = r
            if self.seen[eng].get(sem, 0) >= val:
                continue
            waits[sem] = max(waits.get(sem, 0), val)
        for sem, val in waits.items():
            self.seen[eng][sem] = val
        seq = len(self.ops[eng])
        self.icnt[eng] = cur_inum + 1
        if inc is not None:
            self.counts[inc] += inc_by
            self.marks[eng].append((seq, inc, self.counts[inc]))
            dep = (eng, seq, inc, self.counts[inc], cur_inum)
        else:
            dep = (eng, seq, None, 0, cur_inum)
        self.ops[eng].append((sorted(waits.items()), thunk, inc, inc_by))
        for b in reads:
            b.readers.append(dep)
        for b in writes:
            b.valid = dep
            b.readers = []
        return dep

    def raw_wait(self, eng, sem, val):
        if self.seen[eng].get(sem, 0) < val:
            self.seen[eng][sem] = val
            self.ops[eng].append(([(sem, val)], None, None, 0))


class Buf:
    __slots__ = ("valid", "readers", "name")

    def __init__(self, name=""):
        self.valid = None
        self.readers = []
        self.name = name


def build_program():
    nc = bass.Bass("TRN2", target_bir_lowering=False)
    p = Plan()
    es = ExitStack()

    def din(name, shape, dt=F32):
        return nc.dram_tensor(name, shape, dt, kind="ExternalInput").ap()

    xT = din("xT", [D, T])
    win = din("win", [32, 128, 8, 128])
    wout = din("wout", [8, 128, 16, 128])
    wr = din("wr", [8, 128, 8, 128])
    wk = din("wk", [8, 128, 8, 128])
    wv = din("wv", [8, 128, 8, 128])
    xp = din("xp", [128, 16 * 32])
    sel_in = din("sel", [17, 17 * 128])
    cdiag = din("cdiag", [128, 64 * 128])
    ddiag = din("ddiag", [128, 16 * 128])
    ident_in = din("ident", [128, 128])
    a_par = din("a_par", [128, 256])
    dtw_in = din("dtw", [128, 16])
    dtb_in = din("dtb", [128, 16])
    mur_in = din("mur", [128, 8])
    muk_in = din("muk", [128, 8])
    outb_in = din("outb", [128, 8])

    y_out = nc.dram_tensor("y_out", [D, T], F32, kind="ExternalOutput").ap()

    def scratch(name, shape):
        return nc.dram_tensor(name, shape, BF16, kind="Internal").ap()

    win_bf = scratch("win_bf", [32, 128, 8 * 128])
    wout_bf = scratch("wout_bf", [8, 128, 16 * 128])
    wr_bf = scratch("wr_bf", [8, 128, 8 * 128])
    wk_bf = scratch("wk_bf", [8, 128, 8 * 128])
    wv_bf = scratch("wv_bf", [8, 128, 8 * 128])
    cd_bf = scratch("cd_bf", [128, 64 * 128])
    dd_bf = scratch("dd_bf", [128, 16 * 128])

    def sb(name, shape, dt):
        return es.enter_context(nc.sbuf_tensor(name, shape, dt))

    s_xp = sb("s_xp", [128, 16 * 32], BF16)
    s_ident = sb("s_ident", [128, 128], F32R)
    s_ones128 = sb("s_ones128", [128, 128], BF16)
    s_A = sb("s_A", [128, 256], F32)
    s_dtw = sb("s_dtw", [128, 16], F32)
    s_dtb = sb("s_dtb", [128, 16], F32)
    s_mur = sb("s_mur", [128, 8], F32)
    s_muk = sb("s_muk", [128, 8], F32)
    s_cmur = sb("s_cmur", [128, 8], F32)
    s_cmuk = sb("s_cmuk", [128, 8], F32)
    s_outb = sb("s_outb", [128, 8], F32)
    s_const = sb("s_const", [128, 2], F32)
    s_winb = [sb(f"s_winb{i}", [128, 8 * 128], BF16) for i in range(2)]
    s_wzb = [sb(f"s_wzb{i}", [128, 8 * 128], BF16) for i in range(2)]
    s_woutb = [sb("s_woutb0", [128, 16 * 128], BF16)]
    s_ffnb = [sb(f"s_ffnb{i}", [128, 8 * 128], BF16) for i in range(2)]
    s_cdb = [sb(f"s_cdb{i}", [128, 4 * 128], BF16) for i in range(2)]
    s_ddb = [sb(f"s_ddb{i}", [128, 128], BF16) for i in range(2)]
    s_x = [[sb(f"s_x{pi}_{k}", [128, TC], F32) for k in range(KD)]
           for pi in range(2)]
    s_xbf = [[sb(f"s_xbf{pi}_{k}", [128, TC], BF16) for k in range(KD)]
             for pi in range(2)]
    s_upre = [sb(f"s_upre{i}", [128, TC + 4], BF16) for i in range(2)]
    s_utail = sb("s_utail", [128, 16 * 4], BF16)
    s_u = [sb(f"s_u{m}", [128, TC], BF16) for m in range(KE)]
    s_xdbl = sb("s_xdbl", [17, TC], F32R)
    s_sel = sb("s_sel", [17, 17 * 128], F32R)
    s_dtin = sb("s_dtin", [128, TC], F32)
    s_B = [sb(f"s_B{n}", [128, TC], F32) for n in range(N)]
    s_dl = [sb(f"s_dl{i}", [128, TC], F32) for i in range(4)]
    s_a = [sb(f"s_a{i}", [128, TC], F32) for i in range(2)]
    s_b = [sb(f"s_b{i}", [128, TC], F32) for i in range(4)]
    s_h = [sb(f"s_h{i}", [128, TC], F32) for i in range(6)]
    s_hb = [sb(f"s_hb{i}", [128, TC], F32R) for i in range(2)]
    s_state = sb("s_state", [128, 256], F32)
    s_sz = [sb(f"s_sz{i}", [128, TC], F32) for i in range(2)]
    s_x1 = [sb(f"s_x1{k}", [128, TC + 1], F32) for k in range(KD)]
    s_xr = [sb(f"s_xr{k}", [128, TC], BF16) for k in range(KD)]
    s_xk = [sb(f"s_xk{k}", [128, TC], BF16) for k in range(KD)]

    def psum(name, shape=(128, TC)):
        return es.enter_context(nc.psum_tensor(name, list(shape), F32))

    p_A = [psum("p_A0"), psum("p_A1")]
    p_C = psum("p_C")
    p_X = psum("p_X")
    p_Bc = psum("p_Bc")
    p_Y = [psum("p_Y0"), psum("p_Y1")]
    p_Z = psum("p_Z")

    SM_DMA = p.sem("sm_dma")
    SM_OUT = p.sem("sm_out")
    DSEMS = {}
    for nm in (["d_win0", "d_win1", "d_wz0", "d_wz1", "d_wout", "d_ffn0",
                "d_ffn1", "d_cd0", "d_cd1", "d_dd0", "d_dd1", "d_x0", "d_x1"]
               + [f"d_st{i}" for i in range(8)] + ["d_out0", "d_out1"]):
        DSEMS[nm] = p.sem(nm)
    SM_PE = p.sem("sm_pe")
    SM_ACT = p.sem("sm_act")
    SM_DVE = p.sem("sm_dve")
    SM_POOL = p.sem("sm_pool")
    ESEM = {"tensor": SM_PE, "scalar": SM_ACT, "vector": SM_DVE,
            "gpsimd": SM_POOL}

    B_ = {}

    def buf(key):
        if key not in B_:
            B_[key] = Buf(str(key))
        return B_[key]

    def dma(src_ap, dst_ap, reads=(), writes=(), sem=SM_DMA, q="sync"):
        p.raw_wait(q, sem, p.counts[sem])
        def th(eng, s=src_ap, d=dst_ap):
            return eng.dma_start(out=d, in_=s)
        return p.emit(q, th, reads=reads, writes=writes, inc=sem, inc_by=16)

    def tap(name, ap, reads, chunk):
        if name not in TAPS_WANTED or chunk != TAP_CHUNK:
            return
        t = nc.dram_tensor("tap_" + name, list(ap.shape), ap.dtype,
                           kind="ExternalOutput").ap()
        dma(ap, t, reads=reads, writes=(), sem=SM_OUT)

    def act(fn, out_ap, in_ap, reads, writes, scale=1.0, bias=0.0):
        def th(eng, o=out_ap, i=in_ap, f=fn, s=scale, b=bias):
            return eng.activation(o, i, f, bias=b, scale=s)
        return p.emit("scalar", th, reads=reads, writes=writes, inc=SM_ACT)

    def vop(engn, th, reads, writes, inc=True):
        return p.emit(engn, th, reads=reads, writes=writes,
                      inc=ESEM[engn] if inc else None)

    def mmgroup(bank_buf, bank_ap, items):
        n = len(items)
        for i, (l, r, rd) in enumerate(items):
            def th(eng, o=bank_ap, l=l, r=r, st=(i == 0), sp=(i == n - 1)):
                return eng.matmul(o, l, r, start=st, stop=sp,
                                  skip_group_check=True)
            p.emit("tensor", th, reads=rd,
                   writes=[bank_buf] if (i == 0 or i == n - 1) else (),
                   inc=SM_PE if i == n - 1 else None)

    small_loads = [("a_par", a_par, s_A), ("dtw", dtw_in, s_dtw),
                   ("dtb", dtb_in, s_dtb), ("mur", mur_in, s_mur),
                   ("muk", muk_in, s_muk), ("outb", outb_in, s_outb)]
    for name, src, dst in small_loads:
        dma(src, dst[:], writes=[buf(name)])
    act(AF.Exp, s_A[:], s_A[:], reads=[buf("a_par")], writes=[buf("a_par")])
    vop("vector", (lambda eng: eng.tensor_scalar(
        out=s_A[:], in0=s_A[:], scalar1=-1.0, scalar2=None, op0=OP.mult)),
        reads=[buf("a_par")], writes=[buf("a_par")])
    vop("vector", (lambda eng: eng.tensor_scalar(
        out=s_cmur[:], in0=s_mur[:], scalar1=-1.0, scalar2=1.0,
        op0=OP.mult, op1=OP.add)), reads=[buf("mur")], writes=[buf("cmur")])
    vop("vector", (lambda eng: eng.tensor_scalar(
        out=s_cmuk[:], in0=s_muk[:], scalar1=-1.0, scalar2=1.0,
        op0=OP.mult, op1=OP.add)), reads=[buf("muk")], writes=[buf("cmuk")])
    for off in range(0, 17 * 128, 512):
        w = min(512, 17 * 128 - off)
        lb = buf(("b", 0))
        dma(sel_in[:, off:off + w], s_b[0][:17, :w], writes=[lb])
        vop("vector", (lambda eng, off=off, w=w: eng.tensor_copy(
            out=s_sel[:, off:off + w], in_=s_b[0][:17, :w])),
            reads=[lb], writes=[buf("sel")])
    vop("vector", (lambda eng: eng.memset(s_const[:, 0:1], EPS)), reads=(),
        writes=[buf("const")])
    vop("vector", (lambda eng: eng.memset(s_const[:, 1:2], -0.5)), reads=(),
        writes=[buf("const")])
    vop("vector", (lambda eng: eng.memset(s_ones128[:], 1.0)), reads=(),
        writes=[buf("ones128")])
    vop("vector", (lambda eng: eng.memset(s_utail[:], 0.0)), reads=(),
        writes=[buf(("utail", m)) for m in range(KE)])
    vop("vector", (lambda eng: eng.memset(s_state[:], 0.0)), reads=(),
        writes=[buf(("st", i)) for i in range(256)])
    for k in range(KD):
        vop("vector", (lambda eng, k=k: eng.memset(s_x1[k][:, 0:1], 0.0)),
            reads=(), writes=[buf(("x1", k))])

    lb = buf(("b", 0))
    dma(ident_in, s_b[0][:, :128], writes=[lb])
    vop("vector", (lambda eng: eng.tensor_copy(
        out=s_ident[:], in_=s_b[0][:, :128])), reads=[lb], writes=[buf("ident")])

    lb = buf(("b", 0))
    dma(xp[:, 0:512], s_b[0][:, 0:512], writes=[lb])
    vop("vector", (lambda eng: eng.tensor_copy(
        out=s_xp[:], in_=s_b[0][:, 0:512])), reads=[lb], writes=[buf("xp")])

    _pp_slot = [0]
    WBF = {}

    def cast_blocks(name, blocks):
        for srcap, dstap, w, key in blocks:
            i = _pp_slot[0] % 8
            _pp_slot[0] += 1
            fb = buf(("x", 1, i))
            bb = buf(("xbf", 1, i))
            dma(srcap, s_x[1][i][:, :w], writes=[fb], sem=DSEMS[f"d_st{i}"])
            act(AF.Copy, s_xbf[1][i][:, :w], s_x[1][i][:, :w],
                reads=[fb], writes=[bb])
            dma(s_xbf[1][i][:, :w], dstap, reads=[bb],
                writes=[buf(key)], sem=DSEMS[f"d_st{i}"])

    def wblocks(name, src_f32, dst_bf, nblk, blkw):
        out = []
        for i in range(nblk):
            for off in range(0, blkw, TC):
                w = min(TC, blkw - off)
                s = (src_f32[i].rearrange("p a b -> p (a b)")[:, off:off + w]
                     if src_f32.ndim == 4 else
                     src_f32[:, i * blkw + off:i * blkw + off + w])
                d = (dst_bf[i][:, off:off + w] if dst_bf.ndim == 3
                     else dst_bf[:, i * blkw + off:i * blkw + off + w])
                key = ("wbf", name, i, off)
                WBF.setdefault((name, i), []).append(key)
                out.append((s, d, w, key))
        return out

    cast_blocks("win", wblocks("win", win, win_bf, 32, 1024))
    for o in range(0, 64 * 128, TC):
        WBF.setdefault(("cd", o // TC), []).append(("wbf", "cd", o // TC))
    cast_blocks("cd", [(cdiag[:, o:o + TC], cd_bf[:, o:o + TC], TC,
                        ("wbf", "cd", o // TC)) for o in range(0, 64 * 128, TC)])
    for o in range(0, 16 * 128, TC):
        WBF.setdefault(("dd", o // TC), []).append(("wbf", "dd", o // TC))
    cast_blocks("dd", [(ddiag[:, o:o + TC], dd_bf[:, o:o + TC], TC,
                        ("wbf", "dd", o // TC)) for o in range(0, 16 * 128, TC)])
    cast_blocks("wout", wblocks("wout", wout, wout_bf, 8, 2048))
    cast_blocks("wr", wblocks("wr", wr, wr_bf, 8, 1024))
    cast_blocks("wk", wblocks("wk", wk, wk_bf, 8, 1024))
    cast_blocks("wv", wblocks("wv", wv, wv_bf, 8, 1024))

    for c in range(NCH):
        t0 = c * TC
        pi = c % 2
        sx = s_x[pi]
        sxbf = s_xbf[pi]
        for k in range(KD):
            dma(xT[k * 128:(k + 1) * 128, t0:t0 + TC], sx[k][:],
                writes=[buf(("x", pi, k))], sem=DSEMS[f"d_x{pi}"])
            act(AF.Copy, sxbf[k][:], sx[k][:],
                reads=[buf(("x", pi, k))], writes=[buf(("xbf", pi, k))])

        for m in range(KE):
            wb = buf(("winb", m % 2))
            dma(win_bf[m], s_winb[m % 2][:], writes=[wb],
                reads=[buf(k) for k in WBF[("win", m)]] if c == 0 else (),
                sem=DSEMS[f"d_win{m % 2}"])
            cdb = buf(("cdb", m % 2))
            dma(cd_bf[:, m * 512:(m + 1) * 512], s_cdb[m % 2][:], writes=[cdb],
                reads=[buf(k) for k in WBF[("cd", m)]] if c == 0 else (),
                sem=DSEMS[f"d_cd{m % 2}"])
            bank, pb = p_A[m % 2], buf(("pA", m % 2))
            mmgroup(pb, bank[:],
                    [(s_winb[m % 2][:, k * 128:(k + 1) * 128], sxbf[k][:],
                      [wb, buf(("xbf", pi, k))]) for k in range(KD)])
            ring, rb = s_upre[m % 2], buf(("upre", m % 2))
            rtb = buf(("upre_t", m % 2))
            if m == 0:
                tap("utail_pre0", s_utail[:, 0:4], [buf(("utail", 0))], c)
            act(AF.Copy, ring[:, 0:4], s_utail[:, m * 4:m * 4 + 4],
                reads=[buf(("utail", m))], writes=[rtb])
            act(AF.Copy, ring[:, 4:4 + TC], bank[:], reads=[pb], writes=[rb])
            vop("gpsimd", (lambda eng, r=ring, m=m: eng.tensor_copy(
                out=s_utail[:, m * 4:m * 4 + 4], in_=r[:, TC:TC + 4])),
                reads=[rb], writes=[buf(("utail", m))])
            cb = buf("pC")
            mmgroup(cb, p_C[:],
                    [(s_cdb[m % 2][:, kk * 128:(kk + 1) * 128],
                      ring[:, kk + 1:kk + 1 + TC], [cdb, rb, rtb])
                     for kk in range(4)])
            ub = buf(("u", m))
            act(AF.Silu, s_u[m][:], p_C[:], reads=[cb], writes=[ub])
            tap(f"u{m}", s_u[m][:], [ub], c)
            tap(f"upre{m}", ring[:, 0:TC + 4], [rb, rtb], c)
        xpb = buf("pX")
        mmgroup(xpb, p_X[0:17, :],
                [(s_xp[:, m * 32:m * 32 + 17], s_u[m][:],
                  [buf(("u", m)), buf("xp")]) for m in range(KE)])
        xdb = buf("xdbl")
        vop("vector", (lambda eng: eng.tensor_copy(
            out=s_xdbl[:], in_=p_X[0:17, :])), reads=[xpb], writes=[xdb])
        tap("xdbl", s_xdbl[:], [xdb], c)
        if "xdblf" in TAPS_WANTED and c == TAP_CHUNK:
            vop("vector", (lambda eng: eng.tensor_copy(
                out=s_a[0][:17, :], in_=p_X[0:17, :])), reads=[xpb],
                writes=[buf(("a", 0))])
            tap("xdblf", s_a[0][:17, :], [buf(("a", 0))], c)
        if "sxp" in TAPS_WANTED and c == TAP_CHUNK:
            tap("sxp", s_xp[:], [buf("xp")], c)
        bcb = buf("pBc")
        for r in range(17):
            mmgroup(bcb, p_Bc[:],
                    [(s_sel[:, r * 128:(r + 1) * 128], s_xdbl[:],
                      [xdb, buf("sel")])])
            dstap = s_dtin[:] if r == 0 else s_B[r - 1][:]
            dstbuf = buf("dtin") if r == 0 else buf(("B", r - 1))
            vop("vector", (lambda eng, d=dstap: eng.tensor_copy(
                out=d, in_=p_Bc[:])), reads=[bcb], writes=[dstbuf])
            if r == 0:
                tap("dtin", s_dtin[:], [dstbuf], c)
            else:
                tap(f"B{r-1}", s_B[r - 1][:], [dstbuf], c)

        for m in range(KE):
            sl = m % 2
            dsl = m % 4
            def z_group(mm_):
                zbank, zkey = (p_Z, "pZ") if mm_ % 2 == 0 else (p_Bc, "pBc")
                wzbuf = buf(("wzb", mm_ % 2))
                dma(win_bf[16 + mm_], s_wzb[mm_ % 2][:], writes=[wzbuf],
                    reads=[buf(k) for k in WBF[("win", 16 + mm_)]] if c == 0 else (),
                    sem=DSEMS[f"d_wz{mm_ % 2}"])
                zb = buf(zkey)
                mmgroup(zb, zbank[:],
                        [(s_wzb[mm_ % 2][:, k * 128:(k + 1) * 128],
                          sxbf[k][:],
                          [wzbuf, buf(("xbf", pi, k))]) for k in range(KD)])

            def z_silu(mm_):
                zbank, zkey = (p_Z, "pZ") if mm_ % 2 == 0 else (p_Bc, "pBc")
                szb2 = buf(("sz", mm_ % 2))
                act(AF.Silu, s_sz[mm_ % 2][:], zbank[:], reads=[buf(zkey)],
                    writes=[szb2])
                tap(f"sz{mm_}", s_sz[mm_ % 2][:], [szb2], c)

            if m == 0:
                for mm_ in (0, 1):
                    z_group(mm_)
                    z_silu(mm_)
            szb = buf(("sz", m % 2))

            def emit_delta(mm_, phase):
                dsl_ = mm_ % 4
                db_ = buf(("dl", dsl_))
                if phase is AF.Exp:
                    act(AF.Exp, s_dl[dsl_][:], s_dtin[:],
                        scale=s_dtw[:, mm_:mm_ + 1], bias=s_dtb[:, mm_:mm_ + 1],
                        reads=[buf("dtin"), buf("dtw"), buf("dtb")],
                        writes=[db_])
                else:
                    act(AF.Ln, s_dl[dsl_][:], s_dl[dsl_][:], bias=1.0,
                        reads=[db_], writes=[db_])
                    tap(f"dl{mm_}", s_dl[dsl_][:], [db_], c)

            if m == 0:
                for mm_ in (0, 1):
                    emit_delta(mm_, AF.Exp)
                for mm_ in (0, 1):
                    emit_delta(mm_, AF.Ln)
            dlb = buf(("dl", dsl))
            ddb = buf(("ddb", sl))
            dma(dd_bf[:, m * 128:(m + 1) * 128], s_ddb[sl][:], writes=[ddb],
                reads=[buf(k) for k in WBF[("dd", m // 4)]] if c == 0 else (),
                sem=DSEMS[f"d_dd{sl}"])
            ybank, yb = p_Y[sl], buf(("pY", sl))
            def th_seed(eng, o=ybank[:], l=s_ddb[sl][:], r=s_u[m][:]):
                return eng.matmul(o, l, r, start=True, stop=False,
                                  skip_group_check=True)
            p.emit("tensor", th_seed, reads=[ddb, buf(("u", m))],
                   writes=[yb], inc=None)
            if m % 2 == 0 and m + 2 < KE:
                z_group(m + 2)
                z_group(m + 3)
            pa_slots = [p_A[0], p_A[1], p_C, p_X]
            pa_keys = [("pA", 0), ("pA", 1), "pC", "pX"]

            g2 = 2 * (c * KE + m)

            def emit_hb_sum(n, eng):
                hs_, hbb = (n + g2) % 6, buf(("hb", n % 2))
                vop(eng, (lambda e, n=n, hs=hs_: e.tensor_tensor(
                    out=s_hb[n % 2][:], in0=s_h[hs][:], in1=s_B[n][:],
                    op=OP.mult)),
                    reads=[buf(("h", hs_)), buf(("B", n))], writes=[hbb])
                last = (n == N - 1)
                def th_sum(eng2, o=ybank[:], l=s_ident[:],
                           r=s_hb[n % 2][:], sp=last):
                    return eng2.matmul(o, l, r, start=False, stop=sp,
                                       skip_group_check=True)
                p.emit("tensor", th_sum, reads=[hbb, buf("ident")],
                       writes=[yb] if last else (), inc=SM_PE)

            for n in range(N):
                ns = n % 4
                ab = buf(pa_keys[ns])
                act(AF.Exp, pa_slots[ns][:], s_dl[dsl][:],
                    scale=s_A[:, m * 16 + n:m * 16 + n + 1],
                    reads=[dlb, buf("a_par")], writes=[ab])
                if n == 0 and m + 2 < KE:
                    emit_delta(m + 2, AF.Exp)
                if n == 5 and m + 2 < KE:
                    emit_delta(m + 2, AF.Ln)
                bb = buf(("b", ns))
                vop("gpsimd", (lambda eng, ns=ns, n=n, dsl=dsl: eng.tensor_tensor(
                    out=s_b[ns][:], in0=s_dl[dsl][:], in1=s_B[n][:], op=OP.mult)),
                    reads=[dlb, buf(("B", n))], writes=[bb])
                idx = m * 16 + n
                hs = (n + g2) % 6
                hbuf = buf(("h", hs))
                vop("vector", (lambda eng, ns=ns, hs=hs, idx=idx:
                               eng.tensor_tensor_scan(
                    out=s_h[hs][:], data0=pa_slots[ns][:], data1=s_b[ns][:],
                    initial=s_state[:, idx:idx + 1], op0=OP.mult, op1=OP.add)),
                    reads=[ab, bb, buf(("st", idx))], writes=[hbuf])
                if m == 0 and n == 0:
                    tap("h00", s_h[hs][:], [hbuf], c)
                    tap("b00", s_b[0][:], [bb], c)
                dma(s_h[hs][:, TC - 1:TC], s_state[:, idx:idx + 1],
                    reads=[hbuf], writes=[buf(("st", idx))],
                    sem=DSEMS[f"d_st{idx % 8}"])
                if n >= 3 and n - 3 <= 9:
                    emit_hb_sum(n - 3, "vector")
            for nn in range(10, N):
                emit_hb_sum(nn, "gpsimd")
            ygb = buf(("u", m))
            vop("vector", (lambda eng, m=m, ybank=ybank: eng.tensor_tensor(
                out=s_u[m][:], in0=ybank[:], in1=s_sz[m % 2][:], op=OP.mult)),
                reads=[yb, szb], writes=[ygb])
            tap(f"yg{m}", s_u[m][:], [ygb], c)
            if m % 2 == 1 and m + 2 < KE:
                z_silu(m + 1)
                z_silu(m + 2)

        for k in range(KD):
            wb = buf(("woutb", 0))
            dma(wout_bf[k], s_woutb[0][:], writes=[wb],
                reads=[buf(kk) for kk in WBF[("wout", k)]] if c == 0 else (),
                sem=DSEMS["d_wout"])
            bank, pb = p_A[k % 2], buf(("pA", k % 2))
            mmgroup(pb, bank[:],
                    [(s_woutb[0][:, j * 128:(j + 1) * 128], s_u[j][:],
                      [wb, buf(("u", j))]) for j in range(KE)])
            vop("vector", (lambda eng, k=k, bank=bank, sx=sx:
                           eng.scalar_tensor_tensor(
                out=sx[k][:], in0=bank[:], scalar=s_outb[:, k:k + 1],
                in1=sx[k][:], op0=OP.add, op1=OP.add)),
                reads=[pb, buf(("x", pi, k)), buf("outb")],
                writes=[buf(("x", pi, k))])
            tap(f"xm{k}", sx[k][:], [buf(("x", pi, k))], c)

        def layernorm(src_bufs, src_aps, dst_writer):
            for k in range(KD):
                act(AF.Copy, sxbf[k][:], src_aps[k],
                    reads=[src_bufs[k]], writes=[buf(("xbf", pi, k))])
                act(AF.Square, s_xr[k][:], src_aps[k],
                    reads=[src_bufs[k]], writes=[buf(("xr", k))])
            mmgroup(buf("pBc"), p_Bc[:],
                    [(s_ones128[:], sxbf[k][:], [buf(("xbf", pi, k)),
                                                 buf("ones128")])
                     for k in range(KD)])
            mmgroup(buf("pZ"), p_Z[:],
                    [(s_ones128[:], s_xr[k][:], [buf(("xr", k)), buf("ones128")])
                     for k in range(KD)])
            mb = buf(("b", 0))
            vop("vector", (lambda eng: eng.tensor_scalar(
                out=s_b[0][:], in0=p_Bc[:], scalar1=1.0 / D, scalar2=None,
                op0=OP.mult)), reads=[buf("pBc")], writes=[mb])
            m2b = buf(("b", 1))
            vop("vector", (lambda eng: eng.tensor_tensor(
                out=s_b[1][:], in0=s_b[0][:], in1=s_b[0][:], op=OP.mult)),
                reads=[mb], writes=[m2b])
            vop("vector", (lambda eng: eng.scalar_tensor_tensor(
                out=s_b[1][:], in0=p_Z[:], scalar=1.0 / D, in1=s_b[1][:],
                op0=OP.mult, op1=OP.subtract)),
                reads=[buf("pZ"), m2b], writes=[m2b])
            ib = buf(("sz", 0))
            act(AF.Ln, s_sz[0][:], s_b[1][:], bias=s_const[:, 0:1],
                reads=[m2b, buf("const")], writes=[ib])
            act(AF.Exp, s_sz[0][:], s_sz[0][:], scale=s_const[:, 1:2],
                reads=[ib, buf("const")], writes=[ib])
            for k in range(KD):
                vop("gpsimd", (lambda eng, k=k, a=src_aps[k]: eng.tensor_tensor(
                    out=a, in0=a, in1=s_b[0][:], op=OP.subtract)),
                    reads=[src_bufs[k], mb], writes=[src_bufs[k]])
            for k in range(KD):
                dst_writer(k, src_bufs[k], ib)

        def w1(k, xcb, ib):
            vop("gpsimd", (lambda eng, k=k, sx=sx: eng.tensor_tensor(
                out=s_x1[k][:, 1:1 + TC], in0=sx[k][:], in1=s_sz[0][:],
                op=OP.mult)), reads=[xcb, ib], writes=[buf(("x1", k))])
            tap(f"x1_{k}", s_x1[k][:, 1:1 + TC], [buf(("x1", k))], c)
        layernorm([buf(("x", pi, k)) for k in range(KD)],
                  [sx[k][:] for k in range(KD)], w1)

        mix_slots = [(s_a[0], ("a", 0)), (s_a[1], ("a", 1)),
                     (s_b[2], ("b", 2)), (s_b[3], ("b", 3)),
                     (s_dl[0], ("dl", 0)), (s_dl[1], ("dl", 1)),
                     (s_dl[2], ("dl", 2)), (s_dl[3], ("dl", 3))]
        mix_items = [(k, j) for k in range(KD) for j in (0, 1)]

        def mix_ts(i):
            k, j = mix_items[i]
            st, skey = mix_slots[i % 8]
            cmu = s_cmur if j == 0 else s_cmuk
            cbuf = buf("cmur") if j == 0 else buf("cmuk")
            vop("vector", (lambda eng, k=k, cmu=cmu, st=st: eng.tensor_scalar(
                out=st[:], in0=s_x1[k][:, 0:TC],
                scalar1=cmu[:, k:k + 1], scalar2=None, op0=OP.mult)),
                reads=[buf(("x1", k)), cbuf], writes=[buf(skey)])

        def mix_stt(i):
            k, j = mix_items[i]
            st, skey = mix_slots[i % 8]
            mu = s_mur if j == 0 else s_muk
            mbuf = buf("mur") if j == 0 else buf("muk")
            dst, tag = (s_xr, "xr") if j == 0 else (s_xk, "xk")
            vop("vector", (lambda eng, k=k, mu=mu, dst=dst, st=st:
                           eng.scalar_tensor_tensor(
                out=dst[k][:], in0=s_x1[k][:, 1:1 + TC],
                scalar=mu[:, k:k + 1], in1=st[:],
                op0=OP.mult, op1=OP.add)),
                reads=[buf(("x1", k)), buf(skey), mbuf], writes=[buf((tag, k))])

        for i in range(len(mix_items)):
            mix_ts(i)
            if i >= 4:
                mix_stt(i - 4)
        for i in range(len(mix_items) - 4, len(mix_items)):
            mix_stt(i)

        for k in range(KD):
            wb = buf(("ffnb", k % 2))
            dma(wk_bf[k], s_ffnb[k % 2][:], writes=[wb],
                reads=[buf(kk) for kk in WBF[("wk", k)]] if c == 0 else (),
                sem=DSEMS[f"d_ffn{k % 2}"])
            bank, pb = p_A[k % 2], buf(("pA", k % 2))
            mmgroup(pb, bank[:],
                    [(s_ffnb[k % 2][:, j * 128:(j + 1) * 128], s_xk[j][:],
                      [wb, buf(("xk", j))]) for j in range(KD)])
            krb = buf(("a", k % 2))
            vop("vector", (lambda eng, k=k, bank=bank: eng.tensor_scalar(
                out=s_a[k % 2][:], in0=bank[:], scalar1=0.0, scalar2=None,
                op0=OP.max)), reads=[pb], writes=[krb])
            vop("gpsimd", (lambda eng, k=k: eng.tensor_tensor(
                out=s_u[k][:], in0=s_a[k % 2][:], in1=s_a[k % 2][:],
                op=OP.mult)), reads=[krb], writes=[buf(("u", k))])

        def emit_x2m_add(kk2):
            vop("vector", (lambda eng, k=kk2: eng.tensor_tensor(
                out=s_x1[k][:, 1:1 + TC], in0=s_dl[k % 4][:],
                in1=s_x1[k][:, 1:1 + TC], op=OP.add)),
                reads=[buf(("dl", kk2 % 4)), buf(("x1", kk2))],
                writes=[buf(("x1", kk2))])

        for k in range(KD):
            wbr = buf(("ffnb", k % 2))
            dma(wr_bf[k], s_ffnb[k % 2][:], writes=[wbr],
                reads=[buf(kk) for kk in WBF[("wr", k)]] if c == 0 else (),
                sem=DSEMS[f"d_ffn{k % 2}"])
            rb = buf("pZ")
            mmgroup(rb, p_Z[:],
                    [(s_ffnb[k % 2][:, j * 128:(j + 1) * 128], s_xr[j][:],
                      [wbr, buf(("xr", j))]) for j in range(KD)])
            sgb = buf(("dl", k % 4))
            act(AF.Sigmoid, s_dl[k % 4][:], p_Z[:], reads=[rb], writes=[sgb])
            wbv = buf(("winb", k % 2))
            dma(wv_bf[k], s_winb[k % 2][:], writes=[wbv],
                reads=[buf(kk) for kk in WBF[("wv", k)]] if c == 0 else (),
                sem=DSEMS[f"d_win{k % 2}"])
            bankv, pvb = p_A[k % 2], buf(("pA", k % 2))
            mmgroup(pvb, bankv[:],
                    [(s_winb[k % 2][:, j * 128:(j + 1) * 128], s_u[j][:],
                      [wbv, buf(("u", j))]) for j in range(KD)])
            vop("vector", (lambda eng, k=k, bankv=bankv: eng.tensor_tensor(
                out=s_dl[k % 4][:], in0=s_dl[k % 4][:], in1=bankv[:],
                op=OP.mult)), reads=[sgb, pvb], writes=[sgb])
            vop("vector", (lambda eng, k=k: eng.tensor_copy(
                out=s_x1[k][:, 0:1], in_=s_x1[k][:, TC:TC + 1])),
                reads=[buf(("x1", k))], writes=[buf(("x1", k))])
            if k >= 2:
                emit_x2m_add(k - 2)
        emit_x2m_add(KD - 2)
        emit_x2m_add(KD - 1)

        def w2(k, xcb, ib):
            ob = buf(("h", k % 2))
            vop("vector", (lambda eng, k=k: eng.tensor_tensor(
                out=s_h[k % 2][:], in0=s_x1[k][:, 1:1 + TC], in1=s_sz[0][:],
                op=OP.mult)), reads=[xcb, ib], writes=[ob])
            dma(s_h[k % 2][:], y_out[k * 128:(k + 1) * 128, t0:t0 + TC],
                reads=[ob], writes=(), sem=DSEMS[f"d_out{k % 2}"])
        layernorm([buf(("x1", k)) for k in range(KD)],
                  [s_x1[k][:, 1:1 + TC] for k in range(KD)], w2)

    p.raw_wait("sync", DSEMS["d_out0"], p.counts[DSEMS["d_out0"]])
    p.raw_wait("sync", DSEMS["d_out1"], p.counts[DSEMS["d_out1"]])
    p.raw_wait("sync", SM_OUT, p.counts[SM_OUT])

    sems = {nm: es.enter_context(nc.semaphore(nm)) for nm in p.sem_names}

    with nc.Block() as blk:
        def playback(engname):
            def run(eng):
                for waits, thunk, inc, inc_by in p.ops[engname]:
                    for snm, val in waits:
                        eng.wait_ge(sems[snm], val)
                    if thunk is None:
                        continue
                    instr = thunk(eng)
                    if inc is not None:
                        instr.then_inc(sems[inc], inc_by)
            return run
        blk.sync(playback("sync"))
        blk.tensor(playback("tensor"))
        blk.scalar(playback("scalar"))
        blk.vector(playback("vector"))
        blk.gpsimd(playback("gpsimd"))

    es.close()
    return nc


def prep_core_inputs(inputs, c):
    f32 = np.float32
    x = np.ascontiguousarray(np.asarray(inputs["L0_attn_x"][c], f32).T)
    ip = np.asarray(inputs["L0_attn_in_proj_w"], f32)
    win = np.ascontiguousarray(ip.reshape(32, 128, 8, 128).transpose(0, 3, 2, 1))
    ow = np.asarray(inputs["L0_attn_out_w"], f32)
    wout = np.ascontiguousarray(ow.reshape(8, 128, 16, 128).transpose(0, 3, 2, 1))

    def ffnw(name):
        w = np.asarray(inputs[name], f32)
        return np.ascontiguousarray(w.reshape(8, 128, 8, 128).transpose(0, 3, 2, 1))

    xpw = np.asarray(inputs["L0_attn_x_proj_w"], f32)
    xp3 = np.zeros((128, 16, 32), f32)
    xp3[:, :, :17] = xpw.T.reshape(16, 128, 17).transpose(1, 0, 2)
    xp = np.ascontiguousarray(xp3.reshape(128, 16 * 32))
    cw = np.asarray(inputs["L0_attn_conv_w"], f32).reshape(E, 4)
    cd = np.zeros((128, 64, 128), f32)
    for e_hi in range(16):
        for kk in range(4):
            np.einsum('ii->i', cd[:, e_hi * 4 + kk, :])[:] = \
                cw[e_hi * 128:(e_hi + 1) * 128, kk]
    dp = np.asarray(inputs["L0_attn_D_param"], f32)
    dd = np.zeros((128, 16, 128), f32)
    for e_hi in range(16):
        np.einsum('ii->i', dd[:, e_hi, :])[:] = dp[e_hi * 128:(e_hi + 1) * 128]
    ap_ = np.asarray(inputs["L0_attn_A_param"], f32)
    a_par = np.ascontiguousarray(
        ap_.reshape(16, 128, 16).transpose(1, 0, 2).reshape(128, 256))

    def col16(v):
        return np.ascontiguousarray(np.asarray(v, f32).reshape(16, 128).T)

    def col8(v):
        return np.ascontiguousarray(np.asarray(v, f32).reshape(8, 128).T)

    sel = np.zeros((17, 17, 128), f32)
    for r in range(17):
        sel[r, r, :] = 1.0
    sel = np.ascontiguousarray(sel.transpose(1, 0, 2).reshape(17, 17 * 128))
    return {
        "sel": sel,
        "xT": x, "win": win, "wout": wout,
        "wr": ffnw("L0_ffn_W_r"), "wk": ffnw("L0_ffn_W_k"),
        "wv": ffnw("L0_ffn_W_v"),
        "xp": xp,
        "cdiag": np.ascontiguousarray(cd.reshape(128, 64 * 128)),
        "ddiag": np.ascontiguousarray(dd.reshape(128, 16 * 128)),
        "ident": np.eye(128, dtype=f32),
        "a_par": a_par,
        "dtw": col16(np.asarray(inputs["L0_attn_dt_w"], f32)[:, 0]),
        "dtb": col16(inputs["L0_attn_dt_b"]),
        "mur": col8(inputs["L0_ffn_mu_r"]),
        "muk": col8(inputs["L0_ffn_mu_k"]),
        "outb": col8(inputs["L0_attn_out_b"]),
    }


def kernel(**inputs):
    nc = build_program()
    in_maps = [prep_core_inputs(inputs, c) for c in range(8)]
    res = run_bass_kernel_spmd(nc, in_maps, core_ids=list(range(8)))
    out = np.stack([np.asarray(r["y_out"]) for r in res.results])
    return np.ascontiguousarray(out.transpose(0, 2, 1)).astype(np.float32)
